# version 35
# baseline (speedup 1.0000x reference)
import contextlib
import numpy as np
import ml_dtypes

import concourse.bass as bass
import concourse.mybir as mybir
from concourse.alu_op_type import AluOpType
from concourse.bass_utils import run_bass_kernel_spmd

BF16 = mybir.dt.bfloat16
F32 = mybir.dt.float32
AF = mybir.ActivationFunctionType
AX = mybir.AxisListType.X
bfnp = ml_dtypes.bfloat16

MAGIC_C = 12582912.0
LAG = 2
XS = 6
RB = 6


def host_consts(t_seq: int):
    nt = t_seq // 128
    g = np.arange(t_seq)
    r = (np.float32(1.0) / (g + 1).astype(np.float32)).astype(np.float64)

    tritab = np.zeros((128, nt * 128), dtype=np.float64)
    for i in range(nt):
        rt = r[i * 128:(i + 1) * 128]
        blk = np.where(np.arange(128)[:, None] <= np.arange(128)[None, :],
                       rt[None, :], 0.0)
        tritab[:, i * 128:(i + 1) * 128] = blk
    cartab = np.zeros((64, nt * 128), dtype=np.float64)
    for i in range(nt):
        cartab[:i, i * 128:(i + 1) * 128] = r[i * 128:(i + 1) * 128][None, :]
        cartab[32:32 + i, i * 128:(i + 1) * 128] = r[i * 128:(i + 1) * 128][None, :]
    etab = np.zeros((128, 16), dtype=np.float64)
    for b in range(4):
        etab[:, 4 * b + b] = 1.0
    return {
        "tritab": tritab.astype(bfnp),
        "cartab": cartab.astype(bfnp),
        "etab": etab.astype(bfnp),
        "idbf": np.eye(128).astype(bfnp),
        "id32": np.eye(128, dtype=np.float32),
        "ones2bf": np.ones((2, 128), dtype=bfnp),
        "ones32": np.ones((1, 128), dtype=np.float32),
        "cvec": np.tile(np.array([[1e-5, MAGIC_C, 127.0]], dtype=np.float32),
                        (128, 1)),
    }


class Plan:
    ENGINES = ("sync", "gpsimd", "tensor", "vector", "scalar")

    def __init__(self):
        self.ops = {e: [] for e in self.ENGINES}
        self.cnt = {}
        self.dirty = {e: set() for e in self.ENGINES}
        self.readd = {e: set() for e in self.ENGINES}

    def sem(self, name):
        self.cnt.setdefault(name, 0)
        return name

    def add(self, eng, fn, waits=(), inc=None, inc_n=1, reads=(), writes=(),
            drain_before=False):
        waits = [w for w in waits if w is not None]
        rd, wr = set(reads), set(writes)
        need_drain = (drain_before
                      or (self.dirty[eng] & (rd | wr))
                      or (self.readd[eng] & wr))
        if need_drain and eng in ("vector", "scalar", "gpsimd"):
            self.ops[eng].append(("drain", None, (), None, 0))
            self.dirty[eng].clear()
            self.readd[eng].clear()
        ev = None
        if inc is not None:
            self.cnt[inc] = self.cnt.get(inc, 0) + inc_n
            ev = (inc, self.cnt[inc])
        self.ops[eng].append(("op", fn, tuple(waits), inc, inc_n))
        self.dirty[eng].update(wr)
        self.readd[eng].update(rd)
        return ev

    def emit(self, block, sems):
        for eng_name in self.ENGINES:
            stream = self.ops[eng_name]
            if not stream:
                continue

            def make_body(stream=stream):
                def body(eng):
                    for kind, fn, waits, inc, inc_n in stream:
                        if kind == "drain":
                            eng.drain()
                            continue
                        best = {}
                        for s, v in waits:
                            best[s] = max(best.get(s, 0), v)
                        for s, v in best.items():
                            eng.wait_ge(sems[s], v)
                        instr = fn(eng, sems)
                        if inc is not None:
                            instr.then_inc(sems[inc], inc_n)
                return body
            getattr(block, eng_name)(make_body())


def build_nc(t_seq=2048, n_seq=2, d=1024, n_cores=8):
    nt = t_seq // 128
    ntt = n_seq * nt
    nk = d // 128
    nb = d // 512

    nc = bass.Bass("TRN2", target_bir_lowering=False)

    xc = nc.dram_tensor("xc", [n_seq * t_seq, d], F32, kind="ExternalInput")
    wt = nc.dram_tensor("wt", [d, d], F32, kind="ExternalInput")
    bias = nc.dram_tensor("bias", [1, d], F32, kind="ExternalInput")
    tritab = nc.dram_tensor("tritab", [128, nt * 128], BF16, kind="ExternalInput")
    cartab = nc.dram_tensor("cartab", [64, nt * 128], BF16, kind="ExternalInput")
    etab = nc.dram_tensor("etab", [128, 16], BF16, kind="ExternalInput")
    idbf = nc.dram_tensor("idbf", [128, 128], BF16, kind="ExternalInput")
    id32 = nc.dram_tensor("id32", [128, 128], F32, kind="ExternalInput")
    ones2bf = nc.dram_tensor("ones2bf", [2, 128], BF16, kind="ExternalInput")
    ones32 = nc.dram_tensor("ones32", [1, 128], F32, kind="ExternalInput")
    cvec = nc.dram_tensor("cvec", [128, 3], F32, kind="ExternalInput")
    out = nc.dram_tensor("out", [n_seq * t_seq, d], F32, kind="ExternalOutput")
    cc_in = nc.dram_tensor("cc_in", [1, 8], F32)
    cc_out = nc.dram_tensor("cc_out", [1, 8], F32)

    p = Plan()
    sem_names = (["dC", "dWa", "dWb", "dOa", "dOb", "dM", "cc",
                  "T", "V", "S", "G", "dS0", "dS1", "dC2", "dWs", "dC1",
                  "dK0", "dK1", "dK2", "dK3"] + ["dF%d" % s for s in range(XS)] + ["dB%d" % s for s in range(RB)])
    for s in sem_names:
        p.sem(s)

    stack = contextlib.ExitStack()
    sb = lambda name, shape, dt: stack.enter_context(nc.sbuf_tensor(name, shape, dt))
    psf = lambda name, shape: stack.enter_context(nc.psum_tensor(name, shape, F32))

    with stack:
        y1 = sb("y1", [128, ntt * d], F32)
        xbr = sb("xbr", [128, RB * d], BF16)
        qwt = sb("qwt", [128, nk * d], BF16)
        wob = sb("wob", [128, 2 * d], F32)
        qtmp = sb("qtmp", [128, 2 * d], F32)
        qb = sb("qb", [128, d], BF16)
        qt = sb("qt", [128, 2 * d], BF16)
        stag = qt
        bsbf = sb("bsbf", [64, d], BF16)
        bhib = sb("bhib", [1, d], BF16)
        tritb = sb("tritb", [128, 4 * 128], BF16)
        cartb = sb("cartb", [64, 4 * 128], BF16)
        etb = sb("etb", [128, 16], BF16)
        idbfb = sb("idbfb", [128, 128], BF16)
        id32b = sb("id32b", [128, 128], F32)
        ones2b = sb("ones2b", [2, 128], BF16)
        ones32b = sb("ones32b", [1, 128], F32)
        cvecb = sb("cvecb", [128, 3], F32)
        rowf = sb("rowf", [1, d], F32)
        rblo = sb("rblo", [1, d], BF16)
        rbhl = sb("rbhl", [2, d], BF16)
        wsc = sb("wsc", [128, nk], F32)
        wsum = sb("wsum", [128, 1], F32)
        isc128 = wsum
        wsum1 = sb("wsum1", [1, 1], F32)
        wscale128 = sb("wscale128", [128, 1], F32)
        thr128 = sb("thr128", [128, 1], F32)
        nthr128 = sb("nthr128", [128, 1], F32)
        tm1 = sb("tm1", [128, 1], F32)
        mg128 = tm1
        mx8 = sb("mx8", [1, 8], F32)
        mg1 = sb("mg1", [1, 1], F32)
        inv128 = sb("inv128", [128, 1], F32)
        sout128 = sb("sout128", [128, 1], F32)
        r127c = sb("r127c", [128, 1], F32)
        rsoutc = sb("rsoutc", [128, 1], F32)
        bst = sb("bst", [128, 12], F32)
        mv = sb("mv", [128, 4], F32)
        stdb = sb("stdb", [128, 2], F32)
        rstdb = sb("rstdb", [128, 2], F32)
        nbb = sb("nbb", [128, 2], F32)
        tmax = sb("tmax", [128, ntt], F32)
        rsum = sb("rsum", [128, 8], F32)
        ssq = sb("ssq", [128, 8], F32)
        emc = sb("emc", [128, 8], F32)
        msqc = sb("msqc", [128, 8], F32)
        mvq = sb("mvq", [128, 16], F32)
        stdq = sb("stdq", [128, 8], F32)
        rstdq = sb("rstdq", [128, 8], F32)
        nbq = sb("nbq", [128, 8], F32)

        pA = psf("pA", [128, d])
        pB = psf("pB", [128, d])
        pC = psf("pC", [128, 512])
        pD = psf("pD", [128, 512])
        pE = psf("pE", [128, 512])
        pAB = [pA, pB]
        pCD = [pC, pD]

        def plan_consts():
            ld = []
            for dst, src in ((etb, etab),):
                fn = (lambda e, S, dst=dst, src=src:
                      e.dma_start(out=dst[:, :], in_=src[:, :]))
                ld.append(p.add("sync", fn, inc="dC1", inc_n=16))
            for dst, src in ((id32b, id32), (ones32b, ones32), (cvecb, cvec),
                             (rowf, bias)):
                fn = (lambda e, S, dst=dst, src=src:
                      e.dma_start(out=dst[:, :], in_=src[:, :]))
                ld.append(p.add("sync", fn, inc="dC", inc_n=16))
            ev_sync_consts = ld[-1]
            for dst, src in ((idbfb, idbf), (ones2b, ones2bf)):
                fn = (lambda e, S, dst=dst, src=src:
                      e.dma_start(out=dst[:, :], in_=src[:, :]))
                ld.append(p.add("gpsimd", fn, inc="dC2", inc_n=16))
            return ev_sync_consts

        ev_wld1 = [None] * nk
        ev_wred = [None] * nk

        def plan_wld1(c):
            par = c % 2
            w_waits = [ev_wred[c - 2]] if c >= 2 else []
            ev_wld1[c] = p.add(
                "sync", lambda e, S, c=c, par=par: e.dma_start(
                    out=wob[:, par * d:(par + 1) * d],
                    in_=wt[c * 128:(c + 1) * 128, :]),
                waits=w_waits, inc=("dWa" if par == 0 else "dWb"), inc_n=16)
            ev_wred[c] = p.add(
                "scalar", lambda e, S, c=c, par=par: e.activation(
                    qb[:, :], wob[:, par * d:(par + 1) * d], AF.Abs,
                    bias=0.0, scale=1.0, accum_out=wsc[:, c:c + 1]),
                waits=[ev_wld1[c]], inc="S", reads=("wob%d" % par,),
                writes=("wsc", "qb"))

        ev_xf = [None] * ntt
        ev_cast = [None] * ntt
        ev_tc = [None] * ntt
        ev_bs = [None] * ntt
        ev_bse = {}
        ev_bsd = {}
        ev_attn = [None] * ntt
        ev_r1a = [None] * ntt
        ev_stats1 = [None] * ntt
        ev_sqp1 = [None] * ntt
        ev_q1 = [None] * (ntt // 4 + 1)
        ev_q2 = [None] * (ntt // 4 + 1)
        ev_q3 = [None] * (ntt // 4 + 1)
        ev_sq1 = [None] * ntt
        ev_rstd1 = [None] * ntt
        ev_ln1 = [None] * ntt
        ev_absmax = [None] * ntt

        order = [(e % 2) * nt + e // 2 for e in range(ntt)] if n_seq == 2 \
            else list(range(ntt))

        def plan_xload(ee):
            g = order[ee]
            sq, i = divmod(g, nt)
            waits = [ev_cast[ee - XS]] if ee >= XS else []
            ev_xf[ee] = p.add(
                "sync", lambda e, S, sq=sq, i=i, g=g: e.dma_start(
                    out=y1[:, g * d:(g + 1) * d],
                    in_=xc[sq * t_seq + i * 128: sq * t_seq + (i + 1) * 128, :]),
                waits=waits, inc="dF%d" % (ee % XS), inc_n=16)

        def plan_cast(ee):
            g = order[ee]
            slot = ee % RB
            c_waits = [ev_xf[ee]]
            if ee >= RB:
                c_waits.append(ev_attn[ee - RB])
            if ee % 2 == 0:
                fn = (lambda e, S, slot=slot, g=g: e.tensor_copy(
                    xbr[:, slot * d:(slot + 1) * d],
                    y1[:, g * d:(g + 1) * d]))
                ev_cast[ee] = p.add("vector", fn, waits=c_waits, inc="V",
                                    reads=("y1",), writes=("xb%d" % slot,))
            else:
                fn = (lambda e, S, slot=slot, g=g: e.activation(
                    xbr[:, slot * d:(slot + 1) * d],
                    y1[:, g * d:(g + 1) * d], AF.Copy,
                    bias=0.0, scale=1.0))
                ev_cast[ee] = p.add("scalar", fn, waits=c_waits, inc="S",
                                    reads=("y1",), writes=("xb%d" % slot,))

        def plan_bs(ee):
            g = order[ee]
            sq, i = divmod(g, nt)
            slot = ee % RB
            pK = pCD[sq % 2]
            b_waits = [ev_cast[ee]] + ([("dC1", 16)] if ee < 2 else [])
            if i % 2 == 0 and i >= 2:
                b_waits.append(ev_bse[(sq, i // 2 - 1)])
            def bs_fn(e, S, i=i, slot=slot, pK=pK):
                j = i % 2
                mm = None
                for h in range(nb):
                    b = 2 * h + j
                    mm = e.matmul(pK[0:4, 0:512], etb[:, 4 * b:4 * b + 4],
                                  xbr[:, slot * d + h * 512:
                                      slot * d + h * 512 + 512],
                                  start=(j == 0 and h == 0),
                                  stop=(j == 1 and h == nb - 1))
                return mm
            ev_bs[ee] = p.add("tensor", bs_fn, waits=b_waits, inc="T")
            if i % 2 == 1:
                k = i // 2
                n_pair = len(ev_bse)
                sl = n_pair % 2
                sname = "dS%d" % sl
                bw = [ev_bs[ee], (sname, 32 * (n_pair // 2))]
                def bse_fn(e, S, pK=pK, sl=sl):
                    return e.copy(stag[0:4, sl * 512:(sl + 1) * 512],
                                  pK[0:4, :])
                if sl == 1:
                    bse_fn = (lambda e, S, pK=pK, sl=sl: e.tensor_copy(
                        stag[0:4, sl * 512:(sl + 1) * 512], pK[0:4, :]))
                beng, binc = (("scalar", "S") if sl == 0 else ("vector", "V"))
                ev_bse[(sq, k)] = p.add(beng, bse_fn, waits=bw, inc=binc,
                                        writes=("stag%d" % sl,),
                                        drain_before=True)
                dw = [ev_bse[(sq, k)]]
                r0 = 32 * sq + 2 * k
                p.add("gpsimd", lambda e, S, r0=r0, sl=sl: e.dma_start(
                    out=bsbf[r0:r0 + 2, 0:512],
                    in_=stag[0:2, sl * 512:(sl + 1) * 512]),
                    waits=dw, inc=sname, inc_n=16)
                ev_bsd[(sq, k)] = p.add(
                    "gpsimd", lambda e, S, r0=r0, sl=sl: e.dma_start(
                        out=bsbf[r0:r0 + 2, 512:d],
                        in_=stag[2:4, sl * 512:(sl + 1) * 512]),
                    inc=sname, inc_n=16)

        def plan_tc(ee):
            g = order[ee]
            sq, i = divmod(g, nt)
            sl = ee % 4
            t_waits = [ev_attn[ee - 4]] if ee >= 4 else []
            p.add("gpsimd", lambda e, S, i=i, sl=sl: e.dma_start(
                out=tritb[:, sl * 128:(sl + 1) * 128],
                in_=tritab[:, i * 128:(i + 1) * 128]),
                waits=t_waits, inc="dK%d" % sl, inc_n=16)
            ev_tc[ee] = p.add("gpsimd", lambda e, S, i=i, sl=sl: e.dma_start(
                out=cartb[:, sl * 128:(sl + 1) * 128],
                in_=cartab[:, i * 128:(i + 1) * 128]),
                inc="dK%d" % sl, inc_n=16)

        def plan_attn(ee):
            g = order[ee]
            sq, i = divmod(g, nt)
            slot = ee % RB
            sl = ee % 4
            par = ee % 2
            pX = pAB[par]
            at_waits = [ev_cast[ee], ev_tc[ee]]
            if i > 0:
                at_waits.append(ev_bsd[(sq, (i - 1) // 2)])
            if ee >= 2:
                at_waits.append(ev_r1a[ee - 2])
            def attn_fn(e, S, sq=sq, i=i, slot=slot, sl=sl, pX=pX):
                mm = None
                for h in range(nb):
                    hs = h * 512
                    mm = e.matmul(pX[:, hs:hs + 512],
                                  tritb[:, sl * 128:(sl + 1) * 128],
                                  xbr[:, slot * d + hs:slot * d + hs + 512],
                                  start=True, stop=(i == 0))
                if i > 0:
                    kr = 2 * ((i - 1) // 2 + 1)
                    b0 = 32 * sq
                    for h in range(nb):
                        hs = h * 512
                        mm = e.matmul(pX[:, hs:hs + 512],
                                      cartb[b0:b0 + kr, sl * 128:(sl + 1) * 128],
                                      bsbf[b0:b0 + kr, hs:hs + 512],
                                      start=False, stop=True)
                return mm
            ev_attn[ee] = p.add("tensor", attn_fn, waits=at_waits, inc="T")

        def plan_r1stats(ee):
            g = order[ee]
            par = ee % 2
            col = ee % 8
            pX = pAB[par]
            ev_r1a[ee] = p.add("vector", lambda e, S, pX=pX, g=g,
                               col=col: e.scalar_tensor_tensor(
                y1[:, g * d:(g + 1) * d], pX[:, :], 1.0,
                y1[:, g * d:(g + 1) * d], AluOpType.mult,
                AluOpType.add, accum_out=rsum[:, col:col + 1]),
                waits=[ev_attn[ee]], inc="V",
                writes=("y1", "rsum%d" % col), drain_before=True)
            ev_sqp = p.add("scalar", lambda e, S, g=g, col=col: e.activation(
                qb[:, :], y1[:, g * d:(g + 1) * d], AF.Square, bias=0.0,
                scale=1.0, accum_out=ssq[:, col:col + 1]),
                waits=[ev_r1a[ee]], inc="S", reads=("y1",),
                writes=("qb", "ssq%d" % col))
            ev_sqp1[ee] = ev_sqp

        def plan_smalls1q(Q):
            s = Q % 2
            c0 = 4 * s
            rdt = tuple("rsum%d" % (c0 + j) for j in range(4)) + \
                  tuple("ssq%d" % (c0 + j) for j in range(4))
            def smalls_fn(e, S, s=s, c0=c0):
                e.tensor_scalar(mvq[:, 8 * s:8 * s + 4], rsum[:, c0:c0 + 4],
                                1.0 / d, None, AluOpType.mult)
                e.tensor_scalar(emc[:, c0:c0 + 4], ssq[:, c0:c0 + 4],
                                1.0 / d, None, AluOpType.mult)
                e.drain()
                e.tensor_tensor(msqc[:, c0:c0 + 4], mvq[:, 8 * s:8 * s + 4],
                                mvq[:, 8 * s:8 * s + 4], AluOpType.mult)
                e.drain()
                return e.tensor_tensor(mvq[:, 8 * s + 4:8 * s + 8],
                                       emc[:, c0:c0 + 4],
                                       msqc[:, c0:c0 + 4],
                                       AluOpType.subtract)
            ev_q1[Q] = p.add("vector", smalls_fn,
                             waits=[ev_sqp1[4 * Q + 3]], inc="V",
                             reads=rdt, writes=("mvq%d" % s, "emc", "msqc"),
                             drain_before=True)

        def plan_sq1q(Q):
            s = Q % 2
            ev_q2[Q] = p.add("scalar", lambda e, S, s=s: e.activation(
                stdq[:, 4 * s:4 * s + 4], mvq[:, 8 * s + 4:8 * s + 8],
                AF.Sqrt, bias=cvecb[:, 0:1], scale=1.0),
                waits=[ev_q1[Q]] + ([ev_consts] if Q == 0 else []),
                inc="S", writes=("stdq%d" % s,))

        def plan_rstd1q(Q):
            s = Q % 2
            def rstd_fn(e, S, s=s):
                e.reciprocal(rstdq[:, 4 * s:4 * s + 4],
                             stdq[:, 4 * s:4 * s + 4])
                e.drain()
                e.tensor_tensor(nbq[:, 4 * s:4 * s + 4],
                                mvq[:, 8 * s:8 * s + 4],
                                rstdq[:, 4 * s:4 * s + 4], AluOpType.mult)
                e.drain()
                return e.tensor_scalar(nbq[:, 4 * s:4 * s + 4],
                                       nbq[:, 4 * s:4 * s + 4], -1.0, None,
                                       AluOpType.mult)
            ev_q3[Q] = p.add("vector", rstd_fn, waits=[ev_q2[Q]], inc="V",
                             reads=("stdq%d" % s, "mvq%d" % s),
                             writes=("rstdq%d" % s, "nbq%d" % s),
                             drain_before=True)

        def plan_sqrstd(g, ev_stats, ev_sq_out, ev_rstd_out):
            par = g % 2
            ev_sq_out[g] = p.add("scalar", lambda e, S, par=par: e.activation(
                stdb[:, par:par + 1], mv[:, 2 * par + 1:2 * par + 2], AF.Sqrt,
                bias=cvecb[:, 0:1], scale=1.0),
                waits=[ev_stats[g]] + ([ev_consts] if g < 2 else []), inc="S",
                writes=("std%d" % par,))
            def rstd_fn(e, S, par=par):
                e.reciprocal(rstdb[:, par:par + 1], stdb[:, par:par + 1])
                e.drain()
                return e.tensor_scalar(
                    nbb[:, par:par + 1], mv[:, 2 * par:2 * par + 1],
                    rstdb[:, par:par + 1], -1.0, AluOpType.mult, AluOpType.mult)
            ev_rstd_out[g] = p.add("vector", rstd_fn, waits=[ev_sq_out[g]],
                                   inc="V", reads=("mv%d" % par,),
                                   writes=("rstd%d" % par, "nb%d" % par),
                                   drain_before=True)

        def plan_apply1_only(ee):
            g = order[ee]
            Q = ee // 4
            s = Q % 2
            cj = 4 * s + ee % 4
            ev_ln1[ee] = p.add("scalar", lambda e, S, g=g, cj=cj: e.activation(
                y1[:, g * d:(g + 1) * d], y1[:, g * d:(g + 1) * d],
                AF.Identity, bias=nbq[:, cj:cj + 1],
                scale=rstdq[:, cj:cj + 1]),
                waits=[ev_q3[Q]], inc="S",
                reads=("rstdq%d" % s, "nbq%d" % s), writes=("y1",))

        def plan_absmax_only(ee):
            g = order[ee]
            ev_absmax[ee] = p.add("vector", lambda e, S, g=g: e.tensor_reduce(
                tmax[:, g:g + 1], y1[:, g * d:(g + 1) * d], axis=AX,
                op=AluOpType.max, apply_absolute_value=True),
                waits=[ev_ln1[ee]], inc="V", reads=("y1",), writes=("tmax",),
                drain_before=True)

        def plan_wpass2():
            ev_wsum = p.add("vector", lambda e, S: e.tensor_reduce(
                wsum[:, :], wsc[:, :], axis=AX, op=AluOpType.add),
                waits=[ev_wred[nk - 1]], inc="V", reads=("wsc",),
                writes=("wsum",))
            ev_wrow = p.add("tensor", lambda e, S: e.matmul(
                pE[0:1, 0:128], wsum[:, :], id32b[:, :], start=True, stop=True),
                waits=[ev_wsum, ev_consts], inc="T")
            ev_wsum1 = p.add("vector", lambda e, S: e.tensor_reduce(
                wsum1[:, :], pE[0:1, 0:128], axis=AX, op=AluOpType.add),
                waits=[ev_wrow], inc="V", writes=("wsum1",), drain_before=True)
            ev_wbc = p.add("tensor", lambda e, S: e.matmul(
                pE[:, 128:129], ones32b[:, :], wsum1[:, :], start=True,
                stop=True), waits=[ev_wsum1], inc="T")
            ev_wscale = p.add("vector", lambda e, S: e.tensor_scalar(
                wscale128[:, :], pE[:, 128:129], 1.0 / (d * d), None,
                AluOpType.mult), waits=[ev_wbc], inc="V",
                writes=("wscale128",), drain_before=True)
            ev_thr = p.add("vector", lambda e, S: e.tensor_scalar(
                thr128[:, :], wscale128[:, :], 0.5, None, AluOpType.mult),
                waits=[ev_wscale], inc="V", reads=("wscale128",),
                writes=("thr128",))
            ev_nthr = p.add("vector", lambda e, S: e.tensor_scalar(
                nthr128[:, :], wscale128[:, :], -0.5, None, AluOpType.mult),
                waits=[ev_wscale], inc="V", reads=("wscale128",),
                writes=("nthr128",))
            ev_wq = [None] * nk
            for c in range(nk):
                sl = c % 2
                w_waits = [ev_wq[c - 2]] if c >= 2 else [ev_wred[nk - 2 + sl]]
                ev_wld2 = p.add(
                    "gpsimd", lambda e, S, c=c, sl=sl: e.dma_start(
                        out=wob[:, sl * d:(sl + 1) * d],
                        in_=wt[c * 128:(c + 1) * 128, :]),
                    waits=w_waits, inc="dB%d" % sl, inc_n=16)
                p.add("vector", lambda e, S, sl=sl: e.tensor_scalar(
                    qb[:, :], wob[:, sl * d:(sl + 1) * d], thr128[:, :],
                    None, AluOpType.is_gt),
                    waits=[ev_wld2, ev_thr, ev_nthr],
                    reads=("wob%d" % sl, "thr128"), writes=("qb",))
                ev_wq[c] = p.add("vector", lambda e, S, c=c, sl=sl:
                                 e.scalar_tensor_tensor(
                    qwt[:, c * d:(c + 1) * d], wob[:, sl * d:(sl + 1) * d],
                    nthr128[:, :], qb[:, :], AluOpType.is_lt,
                    AluOpType.subtract),
                    inc="V", reads=("wob%d" % sl, "qb", "nthr128"),
                    writes=("qwt",))
            return ev_wq

        ev_consts = plan_consts()
        for j in range(min(4, ntt)):
            plan_tc(j)
        for j in range(min(3, ntt)):
            plan_xload(j)
        for it in range(ntt + 10):
            if it < ntt:
                plan_cast(it)
            if 0 <= it - 2 < ntt:
                plan_attn(it - 2)
            if 4 <= it + 2 < ntt:
                plan_tc(it + 2)
            if it < ntt:
                plan_bs(it)
            if 0 <= it - 2 < ntt:
                plan_r1stats(it - 2)
            if 3 <= it + 3 < ntt:
                plan_xload(it + 3)
            if it >= 6 and (it - 6) % 4 == 0 and (it - 6) // 4 < ntt // 4:
                plan_smalls1q((it - 6) // 4)
            if it >= 7 and (it - 7) % 4 == 0 and (it - 7) // 4 < ntt // 4:
                plan_sq1q((it - 7) // 4)
            if it >= 8 and (it - 8) % 4 == 0 and (it - 8) // 4 < ntt // 4:
                plan_rstd1q((it - 8) // 4)
            if 0 <= it - 8 < ntt:
                plan_apply1_only(it - 8)
            if 0 <= it - 9 < ntt:
                plan_absmax_only(it - 9)
        def tm1_fn(e, S):
            e.memset(mx8[:, :], 0.0)
            return e.tensor_reduce(tm1[:, :], tmax[:, :], axis=AX,
                                   op=AluOpType.max)
        ev_tm1 = p.add("vector", tm1_fn, waits=[ev_absmax[ntt - 1]], inc="V",
                       reads=("tmax",), writes=("tm1", "mx8"),
                       drain_before=True)
        ev_mrow = p.add("tensor", lambda e, S: e.matmul(
            pE[0:1, 256:384], tm1[:, :], id32b[:, :], start=True, stop=True),
            waits=[ev_tm1], inc="T")
        ev_mx1 = p.add("vector", lambda e, S: e.tensor_reduce(
            mx8[0:1, 0:1], pE[0:1, 256:384], axis=AX, op=AluOpType.max),
            waits=[ev_mrow], inc="V", writes=("mx8",), drain_before=True)
        ev_ccw = p.add("gpsimd", lambda e, S: e.dma_start(
            out=cc_in[:, :], in_=mx8[:, :]), waits=[ev_mx1], inc="dM",
            inc_n=16)
        ev_cc = p.add("gpsimd", lambda e, S: e.collective_compute(
            "AllReduce", mybir.AluOpType.max,
            replica_groups=[list(range(n_cores))],
            ins=[cc_in.ap().opt()], outs=[cc_out.ap().opt()]),
            waits=[ev_ccw], inc="cc")
        for c in range(nk):
            if ev_wld1[c] is None:
                plan_wld1(c)
        ev_wq = plan_wpass2()
        ev_wq_all = ev_wq[nk - 1]
        ev_mgld = p.add("gpsimd", lambda e, S: e.dma_start(
            out=mg1[:, :], in_=cc_out[0:1, 0:1]), waits=[ev_cc], inc="dM",
            inc_n=16)
        ev_mgbc = p.add("tensor", lambda e, S: e.matmul(
            pE[:, 129:130], ones32b[:, :], mg1[:, :], start=True, stop=True),
            waits=[ev_mgld], inc="T")
        def iscale_fn(e, S):
            e.tensor_copy(mg128[:, :], pE[:, 129:130])
            e.reciprocal(r127c[:, :], cvecb[:, 2:3])
            e.drain()
            e.tensor_tensor(isc128[:, :], mg128[:, :], r127c[:, :],
                            AluOpType.mult)
            e.drain()
            e.reciprocal(inv128[:, :], isc128[:, :])
            e.drain()
            e.tensor_tensor(sout128[:, :], wscale128[:, :], isc128[:, :],
                            AluOpType.mult)
            e.drain()
            e.reciprocal(rsoutc[:, :], sout128[:, :])
            e.drain()
            return e.tensor_scalar(rowf[0:1, :], rowf[0:1, :],
                                   rsoutc[0:1, 0:1], None, AluOpType.mult)
        ev_iscale = p.add("vector", iscale_fn, waits=[ev_mgbc, ev_consts],
                          inc="V", writes=("inv128", "sout128", "rowf"),
                          drain_before=True)
        ev_bhi = p.add("scalar", lambda e, S: e.copy(bhib[0:1, :], rowf[0:1, :]),
                       waits=[ev_iscale], inc="S", writes=("rowb",))
        ev_blo = p.add("vector", lambda e, S: e.scalar_tensor_tensor(
            rblo[0:1, :], bhib[0:1, :], -1.0, rowf[0:1, :], AluOpType.mult,
            AluOpType.add), waits=[ev_bhi], inc="V", reads=("rowf",),
            writes=("rowb",), drain_before=True)
        p.add("gpsimd", lambda e, S: e.dma_start(
            out=rbhl[0:1, :], in_=bhib[0:1, :]), waits=[ev_blo], inc="dM",
            inc_n=16)
        ev_seed = p.add("gpsimd", lambda e, S: e.dma_start(
            out=rbhl[1:2, :], in_=rblo[0:1, :]), inc="dM", inc_n=16)

        ev_quant = [None] * ntt
        ev_qsub = [None] * ntt
        ev_trans = [None] * ntt
        ev_evac = [None] * ntt
        ev_ffn = [None] * ntt
        ev_r2 = [None] * ntt
        ev_stats2 = [None] * ntt
        ev_sq2 = [None] * ntt
        ev_rstd2 = [None] * ntt
        ev_ln2 = [None] * ntt
        ev_store = [None] * ntt

        def plan_quant(g):
            par = g % 2
            q_waits = [ev_qsub[g - 2]] if g >= 2 else [ev_iscale]
            ev_quant[g] = p.add("scalar", lambda e, S, g=g, par=par:
                                e.activation(
                qtmp[:, par * d:(par + 1) * d], y1[:, g * d:(g + 1) * d],
                AF.Identity, bias=cvecb[:, 1:2], scale=inv128[:, :]),
                waits=q_waits, inc="S", writes=("tb%d" % par,))

        def plan_qsub(g):
            par = g % 2
            s_waits = [ev_quant[g]]
            if g >= 1:
                s_waits.append(ev_trans[g - 1])
            ev_qsub[g] = p.add("vector", lambda e, S, par=par: e.tensor_scalar(
                qb[:, :], qtmp[:, par * d:(par + 1) * d], -1.0, MAGIC_C,
                AluOpType.mult, AluOpType.add),
                waits=s_waits, inc="V", reads=("tb%d" % par,), writes=("qb",))

        def plan_trans(g):
            t_waits = [ev_qsub[g]] + ([("dC2", 32)] if g == 0 else [])
            if g >= 1:
                t_waits.append(ev_evac[g - 1])
            if g == 0:
                t_waits += [ev_bse[(s, nt // 2 - 1)] for s in range(n_seq)]
            def trans_fn(e, S):
                mm = None
                for c in range(nk):
                    dst = pCD[c // 4]
                    mm = e.matmul(dst[:, (c % 4) * 128:(c % 4) * 128 + 128],
                                  qb[:, c * 128:(c + 1) * 128],
                                  idbfb[:, :], start=True, stop=True)
                return mm
            ev_trans[g] = p.add("tensor", trans_fn, waits=t_waits, inc="T")

        def plan_evac(g):
            par = g % 2
            e_waits = [ev_trans[g]]
            if g >= 2:
                e_waits.append(ev_ffn[g - 2])
            def evac_fn(e, S, par=par):
                e.copy(qt[:, par * d:par * d + 512], pC[:, :])
                return e.copy(qt[:, par * d + 512:(par + 1) * d], pD[:, :])
            ev_evac[g] = p.add("scalar", evac_fn, waits=e_waits, inc="S",
                               writes=("qt%d" % par,))

        def plan_ffn(g):
            par = g % 2
            pX = pAB[par]
            if g >= 2:
                f_waits = [ev_evac[g], ev_r2[g - 2]]
            else:
                f_waits = [ev_evac[g], ev_wq_all, ev_seed, ("dC2", 32),
                           ev_r1a[ntt - 2 + g]]
            def ffn_fn(e, S, par=par, pX=pX):
                mm = None
                for h in range(nb):
                    hs = h * 512
                    e.matmul(pX[:, hs:hs + 512], ones2b[0:2, :],
                             rbhl[0:2, hs:hs + 512], start=True, stop=False)
                    for c in range(nk):
                        mm = e.matmul(pX[:, hs:hs + 512],
                                      qt[:, par * d + c * 128:
                                         par * d + (c + 1) * 128],
                                      qwt[:, c * d + hs:c * d + hs + 512],
                                      start=False, stop=(c == nk - 1))
                return mm
            ev_ffn[g] = p.add("tensor", ffn_fn, waits=f_waits, inc="T")

        def plan_r2stats(g):
            par = g % 2
            pX = pAB[par]
            def r2_fn(e, S, par=par, pX=pX, g=g):
                return e.scalar_tensor_tensor(
                    y1[:, g * d:(g + 1) * d], pX[:, :], sout128[:, :],
                    y1[:, g * d:(g + 1) * d], AluOpType.mult, AluOpType.add)
            ev_r2[g] = p.add("vector", r2_fn, waits=[ev_ffn[g]], inc="V",
                             reads=("sout128",), writes=("y1",),
                             drain_before=True)
            def stats_fn(e, S, g=g, par=par):
                e.bn_stats(bst[:, 0:6], y1[:, g * d:g * d + 512])
                e.bn_stats(bst[:, 6:12], y1[:, g * d + 512:(g + 1) * d])
                e.drain()
                return e.bn_aggr(mv[:, 2 * par:2 * par + 2], bst[:, :])
            ev_stats2[g] = p.add("vector", stats_fn, waits=(),
                                 inc="V", reads=("y1",),
                                 writes=("mv%d" % par, "bst"),
                                 drain_before=True)

        def plan_ln2store(g):
            sq, i = divmod(g, nt)
            par = g % 2
            l_waits = [ev_rstd2[g]]
            if g >= 2:
                l_waits.append(ev_store[g - 2])
            else:
                l_waits.append(ev_wq_all)
            ev_ln2[g] = p.add("scalar", lambda e, S, g=g, par=par:
                              e.activation(
                wob[:, par * d:(par + 1) * d], y1[:, g * d:(g + 1) * d],
                AF.Identity, bias=nbb[:, par:par + 1],
                scale=rstdb[:, par:par + 1]),
                waits=l_waits, inc="S",
                reads=("rstd%d" % par, "nb%d" % par, "y1"),
                writes=("wob%d" % par,))
            row0 = sq * t_seq + i * 128
            ev_store[g] = p.add("gpsimd", lambda e, S, par=par, row0=row0:
                                e.dma_start(
                out=out[row0:row0 + 128, :],
                in_=wob[:, par * d:(par + 1) * d]),
                waits=[ev_ln2[g]], inc=("dOa" if par == 0 else "dOb"),
                inc_n=16)

        plan_quant(0)
        if ntt > 1:
            plan_quant(1)
        plan_qsub(0)
        for it in range(ntt + 3):
            if it < ntt:
                plan_trans(it)
                plan_evac(it)
            if it + 1 < ntt:
                plan_qsub(it + 1)
            if it + 2 < ntt:
                plan_quant(it + 2)
            if 0 <= it - 1 < ntt:
                plan_ffn(it - 1)
            if 0 <= it - 2 < ntt:
                plan_r2stats(it - 2)
            if 0 <= it - 3 < ntt:
                plan_sqrstd(it - 3, ev_stats2, ev_sq2, ev_rstd2)
                plan_ln2store(it - 3)

        p.add("gpsimd", lambda e, S: e.nop(),
              waits=[ev_store[ntt - 1], ev_store[ntt - 2]])

        sems = {}
        semstack = contextlib.ExitStack()
        with semstack:
            for s in p.cnt:
                sems[s] = semstack.enter_context(nc.semaphore(s))
            with nc.Block() as block:
                p.emit(block, sems)

    return nc


def make_in_maps(x, w, b, t_seq, n_seq, n_cores):
    consts = host_consts(t_seq)
    wt = np.ascontiguousarray(w.T)
    brow = np.ascontiguousarray(b.reshape(1, -1))
    n_batch = x.shape[0]
    per = n_batch // n_cores
    in_maps = []
    for c in range(n_cores):
        xcv = np.ascontiguousarray(
            x[c * per:(c + 1) * per].reshape(per * t_seq, x.shape[2]))
        m = {"xc": xcv, "wt": wt, "bias": brow}
        m.update(consts)
        in_maps.append(m)
    return in_maps


def kernel(L0_attn_x, L0_ffn_weight, L0_ffn_bias):
    x = np.asarray(L0_attn_x, dtype=np.float32)
    w = np.asarray(L0_ffn_weight, dtype=np.float32)
    b = np.asarray(L0_ffn_bias, dtype=np.float32)
    n_batch, t_seq, d = x.shape
    n_cores = 8
    n_seq = n_batch // n_cores
    nc = build_nc(t_seq=t_seq, n_seq=n_seq, d=d, n_cores=n_cores)
    in_maps = make_in_maps(x, w, b, t_seq, n_seq, n_cores)
    res = run_bass_kernel_spmd(nc, in_maps, core_ids=list(range(n_cores)))
    outs = [r["out"].reshape(n_seq, t_seq, d) for r in res.results]
    return np.ascontiguousarray(np.concatenate(outs, axis=0))


# version 36
# speedup vs baseline: 1.0464x; 1.0464x over previous
import contextlib
import numpy as np
import ml_dtypes

import concourse.bass as bass
import concourse.mybir as mybir
from concourse.alu_op_type import AluOpType
from concourse.bass_utils import run_bass_kernel_spmd

BF16 = mybir.dt.bfloat16
F32 = mybir.dt.float32
AF = mybir.ActivationFunctionType
AX = mybir.AxisListType.X
bfnp = ml_dtypes.bfloat16

MAGIC_C = 12582912.0
LAG = 2
XS = 6
RB = 6


def host_consts(t_seq: int):
    nt = t_seq // 128
    g = np.arange(t_seq)
    r = (np.float32(1.0) / (g + 1).astype(np.float32)).astype(np.float64)

    tritab = np.zeros((128, nt * 128), dtype=np.float64)
    for i in range(nt):
        rt = r[i * 128:(i + 1) * 128]
        blk = np.where(np.arange(128)[:, None] <= np.arange(128)[None, :],
                       rt[None, :], 0.0)
        tritab[:, i * 128:(i + 1) * 128] = blk
    cartab = np.zeros((64, nt * 128), dtype=np.float64)
    for i in range(nt):
        cartab[:i, i * 128:(i + 1) * 128] = r[i * 128:(i + 1) * 128][None, :]
        cartab[32:32 + i, i * 128:(i + 1) * 128] = r[i * 128:(i + 1) * 128][None, :]
    etab = np.zeros((128, 16), dtype=np.float64)
    for b in range(4):
        etab[:, 4 * b + b] = 1.0
    return {
        "tritab": tritab.astype(bfnp),
        "cartab": cartab.astype(bfnp),
        "etab": etab.astype(bfnp),
        "idbf": np.eye(128).astype(bfnp),
        "id32": np.eye(128, dtype=np.float32),
        "ones2bf": np.ones((2, 128), dtype=bfnp),
        "ones32": np.ones((1, 128), dtype=np.float32),
        "cvec": np.tile(np.array([[1e-5, MAGIC_C, 127.0]], dtype=np.float32),
                        (128, 1)),
    }


class Plan:
    ENGINES = ("sync", "gpsimd", "tensor", "vector", "scalar")

    def __init__(self):
        self.ops = {e: [] for e in self.ENGINES}
        self.cnt = {}
        self.dirty = {e: set() for e in self.ENGINES}
        self.readd = {e: set() for e in self.ENGINES}

    def sem(self, name):
        self.cnt.setdefault(name, 0)
        return name

    def add(self, eng, fn, waits=(), inc=None, inc_n=1, reads=(), writes=(),
            drain_before=False):
        waits = [w for w in waits if w is not None]
        rd, wr = set(reads), set(writes)
        need_drain = (drain_before
                      or (self.dirty[eng] & (rd | wr))
                      or (self.readd[eng] & wr))
        if need_drain and eng in ("vector", "scalar", "gpsimd"):
            self.ops[eng].append(("drain", None, (), None, 0))
            self.dirty[eng].clear()
            self.readd[eng].clear()
        ev = None
        if inc is not None:
            self.cnt[inc] = self.cnt.get(inc, 0) + inc_n
            ev = (inc, self.cnt[inc])
        self.ops[eng].append(("op", fn, tuple(waits), inc, inc_n))
        self.dirty[eng].update(wr)
        self.readd[eng].update(rd)
        return ev

    def emit(self, block, sems):
        for eng_name in self.ENGINES:
            stream = self.ops[eng_name]
            if not stream:
                continue

            def make_body(stream=stream):
                def body(eng):
                    for kind, fn, waits, inc, inc_n in stream:
                        if kind == "drain":
                            eng.drain()
                            continue
                        best = {}
                        for s, v in waits:
                            best[s] = max(best.get(s, 0), v)
                        for s, v in best.items():
                            eng.wait_ge(sems[s], v)
                        instr = fn(eng, sems)
                        if inc is not None:
                            instr.then_inc(sems[inc], inc_n)
                return body
            getattr(block, eng_name)(make_body())


def build_nc(t_seq=2048, n_seq=2, d=1024, n_cores=8):
    nt = t_seq // 128
    ntt = n_seq * nt
    nk = d // 128
    nb = d // 512

    nc = bass.Bass("TRN2", target_bir_lowering=False)

    xc = nc.dram_tensor("xc", [n_seq * t_seq, d], F32, kind="ExternalInput")
    wt = nc.dram_tensor("wt", [d, d], F32, kind="ExternalInput")
    bias = nc.dram_tensor("bias", [1, d], F32, kind="ExternalInput")
    tritab = nc.dram_tensor("tritab", [128, nt * 128], BF16, kind="ExternalInput")
    cartab = nc.dram_tensor("cartab", [64, nt * 128], BF16, kind="ExternalInput")
    etab = nc.dram_tensor("etab", [128, 16], BF16, kind="ExternalInput")
    idbf = nc.dram_tensor("idbf", [128, 128], BF16, kind="ExternalInput")
    id32 = nc.dram_tensor("id32", [128, 128], F32, kind="ExternalInput")
    ones2bf = nc.dram_tensor("ones2bf", [2, 128], BF16, kind="ExternalInput")
    ones32 = nc.dram_tensor("ones32", [1, 128], F32, kind="ExternalInput")
    cvec = nc.dram_tensor("cvec", [128, 3], F32, kind="ExternalInput")
    out = nc.dram_tensor("out", [n_seq * t_seq, d], F32, kind="ExternalOutput")
    cc_in = nc.dram_tensor("cc_in", [1, 8], F32)
    cc_out = nc.dram_tensor("cc_out", [1, 8], F32)

    p = Plan()
    sem_names = (["dC", "dWa", "dWb", "dOa", "dOb", "dM", "cc",
                  "T", "V", "S", "G", "dS0", "dS1", "dC2", "dWs", "dC1",
                  "dK0", "dK1", "dK2", "dK3"] + ["dF%d" % s for s in range(XS)] + ["dB%d" % s for s in range(RB)])
    for s in sem_names:
        p.sem(s)

    stack = contextlib.ExitStack()
    sb = lambda name, shape, dt: stack.enter_context(nc.sbuf_tensor(name, shape, dt))
    psf = lambda name, shape: stack.enter_context(nc.psum_tensor(name, shape, F32))

    with stack:
        y1 = sb("y1", [128, ntt * d], F32)
        xbr = sb("xbr", [128, RB * d], BF16)
        qwt = sb("qwt", [128, nk * d], BF16)
        wob = sb("wob", [128, 2 * d], F32)
        qtmp = sb("qtmp", [128, 2 * d], F32)
        qb = sb("qb", [128, d], BF16)
        qt = sb("qt", [128, 2 * d], BF16)
        stag = qt
        bsbf = sb("bsbf", [64, d], BF16)
        bhib = sb("bhib", [1, d], BF16)
        tritb = sb("tritb", [128, 4 * 128], BF16)
        cartb = sb("cartb", [64, 4 * 128], BF16)
        etb = sb("etb", [128, 16], BF16)
        idbfb = sb("idbfb", [128, 128], BF16)
        id32b = sb("id32b", [128, 128], F32)
        ones2b = sb("ones2b", [2, 128], BF16)
        ones32b = sb("ones32b", [1, 128], F32)
        cvecb = sb("cvecb", [128, 3], F32)
        rowf = sb("rowf", [1, d], F32)
        rblo = sb("rblo", [1, d], BF16)
        rbhl = sb("rbhl", [2, d], BF16)
        wsc = sb("wsc", [128, nk], F32)
        wsum = sb("wsum", [128, 1], F32)
        isc128 = wsum
        wsum1 = sb("wsum1", [1, 1], F32)
        wscale128 = sb("wscale128", [128, 1], F32)
        thr128 = sb("thr128", [128, 1], F32)
        nthr128 = sb("nthr128", [128, 1], F32)
        tm1 = sb("tm1", [128, 1], F32)
        mg128 = tm1
        mx8 = sb("mx8", [1, 8], F32)
        mg1 = sb("mg1", [1, 1], F32)
        inv128 = sb("inv128", [128, 1], F32)
        sout128 = sb("sout128", [128, 1], F32)
        r127c = sb("r127c", [128, 1], F32)
        rsoutc = sb("rsoutc", [128, 1], F32)
        bst = sb("bst", [128, 12], F32)
        mv = sb("mv", [128, 4], F32)
        stdb = sb("stdb", [128, 2], F32)
        rstdb = sb("rstdb", [128, 2], F32)
        nbb = sb("nbb", [128, 2], F32)
        tmax = sb("tmax", [128, ntt], F32)
        rsum = sb("rsum", [128, 8], F32)
        ssq = sb("ssq", [128, 8], F32)
        emc = sb("emc", [128, 8], F32)
        msqc = sb("msqc", [128, 8], F32)
        mvq = sb("mvq", [128, 16], F32)
        stdq = sb("stdq", [128, 8], F32)
        rstdq = sb("rstdq", [128, 8], F32)
        nbq = sb("nbq", [128, 8], F32)

        pA = psf("pA", [128, d])
        pB = psf("pB", [128, d])
        pC = psf("pC", [128, 512])
        pD = psf("pD", [128, 512])
        pE = psf("pE", [128, 512])
        pAB = [pA, pB]
        pCD = [pC, pD]

        def plan_consts():
            ld = []
            for dst, src in ((etb, etab),):
                fn = (lambda e, S, dst=dst, src=src:
                      e.dma_start(out=dst[:, :], in_=src[:, :]))
                ld.append(p.add("sync", fn, inc="dC1", inc_n=16))
            for dst, src in ((id32b, id32), (ones32b, ones32), (cvecb, cvec),
                             (rowf, bias)):
                fn = (lambda e, S, dst=dst, src=src:
                      e.dma_start(out=dst[:, :], in_=src[:, :]))
                ld.append(p.add("sync", fn, inc="dC", inc_n=16))
            ev_sync_consts = ld[-1]
            for dst, src in ((idbfb, idbf), (ones2b, ones2bf)):
                fn = (lambda e, S, dst=dst, src=src:
                      e.dma_start(out=dst[:, :], in_=src[:, :]))
                ld.append(p.add("gpsimd", fn, inc="dC2", inc_n=16))
            return ev_sync_consts

        ev_wld1 = [None] * nk
        ev_wred = [None] * nk

        def plan_wld1(c):
            par = c % 2
            w_waits = [ev_wred[c - 2]] if c >= 2 else []
            ev_wld1[c] = p.add(
                "sync", lambda e, S, c=c, par=par: e.dma_start(
                    out=wob[:, par * d:(par + 1) * d],
                    in_=wt[c * 128:(c + 1) * 128, :]),
                waits=w_waits, inc=("dWa" if par == 0 else "dWb"), inc_n=16)
            ev_wred[c] = p.add(
                "scalar", lambda e, S, c=c, par=par: e.activation(
                    qb[:, :], wob[:, par * d:(par + 1) * d], AF.Abs,
                    bias=0.0, scale=1.0, accum_out=wsc[:, c:c + 1]),
                waits=[ev_wld1[c]], inc="S", reads=("wob%d" % par,),
                writes=("wsc", "qb"))

        ev_xf = [None] * ntt
        ev_cast = [None] * ntt
        ev_tc = [None] * ntt
        ev_bs = [None] * ntt
        ev_bse = {}
        ev_bsd = {}
        ev_attn = [None] * ntt
        ev_r1a = [None] * ntt
        ev_stats1 = [None] * ntt
        ev_sqp1 = [None] * ntt
        ev_q1 = [None] * (ntt // 4 + 1)
        ev_q2 = [None] * (ntt // 4 + 1)
        ev_q3 = [None] * (ntt // 4 + 1)
        ev_sq1 = [None] * ntt
        ev_rstd1 = [None] * ntt
        ev_ln1 = [None] * ntt
        ev_absmax = [None] * ntt

        order = [(e % 2) * nt + e // 2 for e in range(ntt)] if n_seq == 2 \
            else list(range(ntt))

        def plan_xload(ee):
            g = order[ee]
            sq, i = divmod(g, nt)
            waits = [ev_cast[ee - XS]] if ee >= XS else []
            ev_xf[ee] = p.add(
                "sync", lambda e, S, sq=sq, i=i, g=g: e.dma_start(
                    out=y1[:, g * d:(g + 1) * d],
                    in_=xc[sq * t_seq + i * 128: sq * t_seq + (i + 1) * 128, :]),
                waits=waits, inc="dF%d" % (ee % XS), inc_n=16)

        def plan_cast(ee):
            g = order[ee]
            slot = ee % RB
            c_waits = [ev_xf[ee]]
            if ee >= RB:
                c_waits.append(ev_attn[ee - RB])
            if ee % 2 == 0:
                fn = (lambda e, S, slot=slot, g=g: e.tensor_copy(
                    xbr[:, slot * d:(slot + 1) * d],
                    y1[:, g * d:(g + 1) * d]))
                ev_cast[ee] = p.add("vector", fn, waits=c_waits, inc="V",
                                    reads=("y1_%d" % g,),
                                    writes=("xb%d" % slot,))
            else:
                fn = (lambda e, S, slot=slot, g=g: e.activation(
                    xbr[:, slot * d:(slot + 1) * d],
                    y1[:, g * d:(g + 1) * d], AF.Copy,
                    bias=0.0, scale=1.0))
                ev_cast[ee] = p.add("scalar", fn, waits=c_waits, inc="S",
                                    reads=("y1_%d" % g,),
                                    writes=("xb%d" % slot,))

        def plan_bs(ee):
            g = order[ee]
            sq, i = divmod(g, nt)
            slot = ee % RB
            pK = pCD[sq % 2]
            b_waits = [ev_cast[ee]] + ([("dC1", 16)] if ee < 2 else [])
            if i % 2 == 0 and i >= 2:
                b_waits.append(ev_bse[(sq, i // 2 - 1)])
            def bs_fn(e, S, i=i, slot=slot, pK=pK):
                j = i % 2
                mm = None
                for h in range(nb):
                    b = 2 * h + j
                    mm = e.matmul(pK[0:4, 0:512], etb[:, 4 * b:4 * b + 4],
                                  xbr[:, slot * d + h * 512:
                                      slot * d + h * 512 + 512],
                                  start=(j == 0 and h == 0),
                                  stop=(j == 1 and h == nb - 1))
                return mm
            ev_bs[ee] = p.add("tensor", bs_fn, waits=b_waits, inc="T")
            if i % 2 == 1:
                k = i // 2
                n_pair = len(ev_bse)
                sl = n_pair % 2
                sname = "dS%d" % sl
                bw = [ev_bs[ee], (sname, 32 * (n_pair // 2))]
                def bse_fn(e, S, pK=pK, sl=sl):
                    return e.copy(stag[0:4, sl * 512:(sl + 1) * 512],
                                  pK[0:4, :])
                if sl == 1:
                    bse_fn = (lambda e, S, pK=pK, sl=sl: e.tensor_copy(
                        stag[0:4, sl * 512:(sl + 1) * 512], pK[0:4, :]))
                beng, binc = (("scalar", "S") if sl == 0 else ("vector", "V"))
                ev_bse[(sq, k)] = p.add(beng, bse_fn, waits=bw, inc=binc,
                                        writes=("stag%d" % sl,),
                                        drain_before=True)
                dw = [ev_bse[(sq, k)]]
                r0 = 32 * sq + 2 * k
                p.add("gpsimd", lambda e, S, r0=r0, sl=sl: e.dma_start(
                    out=bsbf[r0:r0 + 2, 0:512],
                    in_=stag[0:2, sl * 512:(sl + 1) * 512]),
                    waits=dw, inc=sname, inc_n=16)
                ev_bsd[(sq, k)] = p.add(
                    "gpsimd", lambda e, S, r0=r0, sl=sl: e.dma_start(
                        out=bsbf[r0:r0 + 2, 512:d],
                        in_=stag[2:4, sl * 512:(sl + 1) * 512]),
                    inc=sname, inc_n=16)

        def plan_tc(ee):
            g = order[ee]
            sq, i = divmod(g, nt)
            sl = ee % 4
            t_waits = [ev_attn[ee - 4]] if ee >= 4 else []
            p.add("gpsimd", lambda e, S, i=i, sl=sl: e.dma_start(
                out=tritb[:, sl * 128:(sl + 1) * 128],
                in_=tritab[:, i * 128:(i + 1) * 128]),
                waits=t_waits, inc="dK%d" % sl, inc_n=16)
            ev_tc[ee] = p.add("gpsimd", lambda e, S, i=i, sl=sl: e.dma_start(
                out=cartb[:, sl * 128:(sl + 1) * 128],
                in_=cartab[:, i * 128:(i + 1) * 128]),
                inc="dK%d" % sl, inc_n=16)

        def plan_attn(ee):
            g = order[ee]
            sq, i = divmod(g, nt)
            slot = ee % RB
            sl = ee % 4
            par = ee % 2
            pX = pAB[par]
            at_waits = [ev_cast[ee], ev_tc[ee]]
            if i > 0:
                at_waits.append(ev_bsd[(sq, (i - 1) // 2)])
            if ee >= 2:
                at_waits.append(ev_r1a[ee - 2])
            def attn_fn(e, S, sq=sq, i=i, slot=slot, sl=sl, pX=pX):
                mm = None
                for h in range(nb):
                    hs = h * 512
                    mm = e.matmul(pX[:, hs:hs + 512],
                                  tritb[:, sl * 128:(sl + 1) * 128],
                                  xbr[:, slot * d + hs:slot * d + hs + 512],
                                  start=True, stop=(i == 0))
                if i > 0:
                    kr = 2 * ((i - 1) // 2 + 1)
                    b0 = 32 * sq
                    for h in range(nb):
                        hs = h * 512
                        mm = e.matmul(pX[:, hs:hs + 512],
                                      cartb[b0:b0 + kr, sl * 128:(sl + 1) * 128],
                                      bsbf[b0:b0 + kr, hs:hs + 512],
                                      start=False, stop=True)
                return mm
            ev_attn[ee] = p.add("tensor", attn_fn, waits=at_waits, inc="T")

        def plan_r1stats(ee):
            g = order[ee]
            par = ee % 2
            col = ee % 8
            pX = pAB[par]
            ev_r1a[ee] = p.add("vector", lambda e, S, pX=pX, g=g,
                               col=col: e.scalar_tensor_tensor(
                y1[:, g * d:(g + 1) * d], pX[:, :], 1.0,
                y1[:, g * d:(g + 1) * d], AluOpType.mult,
                AluOpType.add, accum_out=rsum[:, col:col + 1]),
                waits=[ev_attn[ee]], inc="V",
                writes=("y1_%d" % g, "rsum%d" % col))
            ev_sqp = p.add("scalar", lambda e, S, g=g, col=col: e.activation(
                qb[:, :], y1[:, g * d:(g + 1) * d], AF.Square, bias=0.0,
                scale=1.0, accum_out=ssq[:, col:col + 1]),
                waits=[ev_r1a[ee]], inc="S", reads=("y1_%d" % g,),
                writes=("qb", "ssq%d" % col))
            ev_sqp1[ee] = ev_sqp

        def plan_smalls1q(Q):
            s = Q % 2
            c0 = 4 * s
            rdt = tuple("rsum%d" % (c0 + j) for j in range(4)) + \
                  tuple("ssq%d" % (c0 + j) for j in range(4))
            def smalls_fn(e, S, s=s, c0=c0):
                e.tensor_scalar(mvq[:, 8 * s:8 * s + 4], rsum[:, c0:c0 + 4],
                                1.0 / d, None, AluOpType.mult)
                e.tensor_scalar(emc[:, c0:c0 + 4], ssq[:, c0:c0 + 4],
                                1.0 / d, None, AluOpType.mult)
                e.drain()
                e.tensor_tensor(msqc[:, c0:c0 + 4], mvq[:, 8 * s:8 * s + 4],
                                mvq[:, 8 * s:8 * s + 4], AluOpType.mult)
                e.drain()
                return e.tensor_tensor(mvq[:, 8 * s + 4:8 * s + 8],
                                       emc[:, c0:c0 + 4],
                                       msqc[:, c0:c0 + 4],
                                       AluOpType.subtract)
            ev_q1[Q] = p.add("vector", smalls_fn,
                             waits=[ev_sqp1[4 * Q + 3]], inc="V",
                             reads=rdt, writes=("mvq%d" % s, "emc", "msqc"))

        def plan_sq1q(Q):
            s = Q % 2
            ev_q2[Q] = p.add("scalar", lambda e, S, s=s: e.activation(
                stdq[:, 4 * s:4 * s + 4], mvq[:, 8 * s + 4:8 * s + 8],
                AF.Sqrt, bias=cvecb[:, 0:1], scale=1.0),
                waits=[ev_q1[Q]] + ([ev_consts] if Q == 0 else []),
                inc="S", writes=("stdq%d" % s,))

        def plan_rstd1q(Q):
            s = Q % 2
            def rstd_fn(e, S, s=s):
                e.reciprocal(rstdq[:, 4 * s:4 * s + 4],
                             stdq[:, 4 * s:4 * s + 4])
                e.drain()
                e.tensor_tensor(nbq[:, 4 * s:4 * s + 4],
                                mvq[:, 8 * s:8 * s + 4],
                                rstdq[:, 4 * s:4 * s + 4], AluOpType.mult)
                e.drain()
                return e.tensor_scalar(nbq[:, 4 * s:4 * s + 4],
                                       nbq[:, 4 * s:4 * s + 4], -1.0, None,
                                       AluOpType.mult)
            ev_q3[Q] = p.add("vector", rstd_fn, waits=[ev_q2[Q]], inc="V",
                             reads=("stdq%d" % s, "mvq%d" % s),
                             writes=("rstdq%d" % s, "nbq%d" % s),
                             drain_before=True)

        def plan_sqrstd(g, ev_stats, ev_sq_out, ev_rstd_out):
            par = g % 2
            ev_sq_out[g] = p.add("scalar", lambda e, S, par=par: e.activation(
                stdb[:, par:par + 1], mv[:, 2 * par + 1:2 * par + 2], AF.Sqrt,
                bias=cvecb[:, 0:1], scale=1.0),
                waits=[ev_stats[g]] + ([ev_consts] if g < 2 else []), inc="S",
                writes=("std%d" % par,))
            def rstd_fn(e, S, par=par):
                e.reciprocal(rstdb[:, par:par + 1], stdb[:, par:par + 1])
                e.drain()
                return e.tensor_scalar(
                    nbb[:, par:par + 1], mv[:, 2 * par:2 * par + 1],
                    rstdb[:, par:par + 1], -1.0, AluOpType.mult, AluOpType.mult)
            ev_rstd_out[g] = p.add("vector", rstd_fn, waits=[ev_sq_out[g]],
                                   inc="V", reads=("mv%d" % par,),
                                   writes=("rstd%d" % par, "nb%d" % par),
                                   drain_before=True)

        def plan_apply1_only(ee):
            g = order[ee]
            Q = ee // 4
            s = Q % 2
            cj = 4 * s + ee % 4
            ev_ln1[ee] = p.add("scalar", lambda e, S, g=g, cj=cj: e.activation(
                y1[:, g * d:(g + 1) * d], y1[:, g * d:(g + 1) * d],
                AF.Identity, bias=nbq[:, cj:cj + 1],
                scale=rstdq[:, cj:cj + 1]),
                waits=[ev_q3[Q]], inc="S",
                reads=("rstdq%d" % s, "nbq%d" % s), writes=("y1_%d" % g,))

        def plan_absmax_only(ee):
            g = order[ee]
            ev_absmax[ee] = p.add("vector", lambda e, S, g=g: e.tensor_reduce(
                tmax[:, g:g + 1], y1[:, g * d:(g + 1) * d], axis=AX,
                op=AluOpType.max, apply_absolute_value=True),
                waits=[ev_ln1[ee]], inc="V", reads=("y1_%d" % g,),
                writes=("tmax",))

        def plan_wpass2():
            ev_wsum = p.add("vector", lambda e, S: e.tensor_reduce(
                wsum[:, :], wsc[:, :], axis=AX, op=AluOpType.add),
                waits=[ev_wred[nk - 1]], inc="V", reads=("wsc",),
                writes=("wsum",))
            ev_wrow = p.add("tensor", lambda e, S: e.matmul(
                pE[0:1, 0:128], wsum[:, :], id32b[:, :], start=True, stop=True),
                waits=[ev_wsum, ev_consts], inc="T")
            ev_wsum1 = p.add("vector", lambda e, S: e.tensor_reduce(
                wsum1[:, :], pE[0:1, 0:128], axis=AX, op=AluOpType.add),
                waits=[ev_wrow], inc="V", writes=("wsum1",), drain_before=True)
            ev_wbc = p.add("tensor", lambda e, S: e.matmul(
                pE[:, 128:129], ones32b[:, :], wsum1[:, :], start=True,
                stop=True), waits=[ev_wsum1], inc="T")
            ev_wscale = p.add("vector", lambda e, S: e.tensor_scalar(
                wscale128[:, :], pE[:, 128:129], 1.0 / (d * d), None,
                AluOpType.mult), waits=[ev_wbc], inc="V",
                writes=("wscale128",), drain_before=True)
            ev_thr = p.add("vector", lambda e, S: e.tensor_scalar(
                thr128[:, :], wscale128[:, :], 0.5, None, AluOpType.mult),
                waits=[ev_wscale], inc="V", reads=("wscale128",),
                writes=("thr128",))
            ev_nthr = p.add("vector", lambda e, S: e.tensor_scalar(
                nthr128[:, :], wscale128[:, :], -0.5, None, AluOpType.mult),
                waits=[ev_wscale], inc="V", reads=("wscale128",),
                writes=("nthr128",))
            ev_wq = [None] * nk
            for c in range(nk):
                sl = c % 2
                w_waits = [ev_wq[c - 2]] if c >= 2 else [ev_wred[nk - 2 + sl]]
                ev_wld2 = p.add(
                    "gpsimd", lambda e, S, c=c, sl=sl: e.dma_start(
                        out=wob[:, sl * d:(sl + 1) * d],
                        in_=wt[c * 128:(c + 1) * 128, :]),
                    waits=w_waits, inc="dB%d" % sl, inc_n=16)
                p.add("vector", lambda e, S, sl=sl: e.tensor_scalar(
                    qb[:, :], wob[:, sl * d:(sl + 1) * d], thr128[:, :],
                    None, AluOpType.is_gt),
                    waits=[ev_wld2, ev_thr, ev_nthr],
                    reads=("wob%d" % sl, "thr128"), writes=("qb",))
                ev_wq[c] = p.add("vector", lambda e, S, c=c, sl=sl:
                                 e.scalar_tensor_tensor(
                    qwt[:, c * d:(c + 1) * d], wob[:, sl * d:(sl + 1) * d],
                    nthr128[:, :], qb[:, :], AluOpType.is_lt,
                    AluOpType.subtract),
                    inc="V", reads=("wob%d" % sl, "qb", "nthr128"),
                    writes=("qwt",))
            return ev_wq

        ev_consts = plan_consts()
        for j in range(min(4, ntt)):
            plan_tc(j)
        for j in range(min(3, ntt)):
            plan_xload(j)
        for it in range(ntt + 10):
            if it < ntt:
                plan_cast(it)
            if 0 <= it - 2 < ntt:
                plan_attn(it - 2)
            if 4 <= it + 2 < ntt:
                plan_tc(it + 2)
            if it < ntt:
                plan_bs(it)
            if 0 <= it - 2 < ntt:
                plan_r1stats(it - 2)
            if 3 <= it + 3 < ntt:
                plan_xload(it + 3)
            if it >= 6 and (it - 6) % 4 == 0 and (it - 6) // 4 < ntt // 4:
                plan_smalls1q((it - 6) // 4)
            if it >= 7 and (it - 7) % 4 == 0 and (it - 7) // 4 < ntt // 4:
                plan_sq1q((it - 7) // 4)
            if it >= 8 and (it - 8) % 4 == 0 and (it - 8) // 4 < ntt // 4:
                plan_rstd1q((it - 8) // 4)
            if 0 <= it - 8 < ntt:
                plan_apply1_only(it - 8)
            if 0 <= it - 9 < ntt:
                plan_absmax_only(it - 9)
        def tm1_fn(e, S):
            e.memset(mx8[:, :], 0.0)
            return e.tensor_reduce(tm1[:, :], tmax[:, :], axis=AX,
                                   op=AluOpType.max)
        ev_tm1 = p.add("vector", tm1_fn, waits=[ev_absmax[ntt - 1]], inc="V",
                       reads=("tmax",), writes=("tm1", "mx8"),
                       drain_before=True)
        ev_mrow = p.add("tensor", lambda e, S: e.matmul(
            pE[0:1, 256:384], tm1[:, :], id32b[:, :], start=True, stop=True),
            waits=[ev_tm1], inc="T")
        ev_mx1 = p.add("vector", lambda e, S: e.tensor_reduce(
            mx8[0:1, 0:1], pE[0:1, 256:384], axis=AX, op=AluOpType.max),
            waits=[ev_mrow], inc="V", writes=("mx8",), drain_before=True)
        ev_ccw = p.add("gpsimd", lambda e, S: e.dma_start(
            out=cc_in[:, :], in_=mx8[:, :]), waits=[ev_mx1], inc="dM",
            inc_n=16)
        ev_cc = p.add("gpsimd", lambda e, S: e.collective_compute(
            "AllReduce", mybir.AluOpType.max,
            replica_groups=[list(range(n_cores))],
            ins=[cc_in.ap().opt()], outs=[cc_out.ap().opt()]),
            waits=[ev_ccw], inc="cc")
        for c in range(nk):
            if ev_wld1[c] is None:
                plan_wld1(c)
        ev_wq = plan_wpass2()
        ev_wq_all = ev_wq[nk - 1]
        ev_mgld = p.add("gpsimd", lambda e, S: e.dma_start(
            out=mg1[:, :], in_=cc_out[0:1, 0:1]), waits=[ev_cc], inc="dM",
            inc_n=16)
        ev_mgbc = p.add("tensor", lambda e, S: e.matmul(
            pE[:, 129:130], ones32b[:, :], mg1[:, :], start=True, stop=True),
            waits=[ev_mgld], inc="T")
        def iscale_fn(e, S):
            e.tensor_copy(mg128[:, :], pE[:, 129:130])
            e.reciprocal(r127c[:, :], cvecb[:, 2:3])
            e.drain()
            e.tensor_tensor(isc128[:, :], mg128[:, :], r127c[:, :],
                            AluOpType.mult)
            e.drain()
            e.reciprocal(inv128[:, :], isc128[:, :])
            e.drain()
            e.tensor_tensor(sout128[:, :], wscale128[:, :], isc128[:, :],
                            AluOpType.mult)
            e.drain()
            e.reciprocal(rsoutc[:, :], sout128[:, :])
            e.drain()
            return e.tensor_scalar(rowf[0:1, :], rowf[0:1, :],
                                   rsoutc[0:1, 0:1], None, AluOpType.mult)
        ev_iscale = p.add("vector", iscale_fn, waits=[ev_mgbc, ev_consts],
                          inc="V", writes=("inv128", "sout128", "rowf"),
                          drain_before=True)
        ev_bhi = p.add("scalar", lambda e, S: e.copy(bhib[0:1, :], rowf[0:1, :]),
                       waits=[ev_iscale], inc="S", writes=("rowb",))
        ev_blo = p.add("vector", lambda e, S: e.scalar_tensor_tensor(
            rblo[0:1, :], bhib[0:1, :], -1.0, rowf[0:1, :], AluOpType.mult,
            AluOpType.add), waits=[ev_bhi], inc="V", reads=("rowf",),
            writes=("rowb",), drain_before=True)
        p.add("gpsimd", lambda e, S: e.dma_start(
            out=rbhl[0:1, :], in_=bhib[0:1, :]), waits=[ev_blo], inc="dM",
            inc_n=16)
        ev_seed = p.add("gpsimd", lambda e, S: e.dma_start(
            out=rbhl[1:2, :], in_=rblo[0:1, :]), inc="dM", inc_n=16)

        ev_quant = [None] * ntt
        ev_qsub = [None] * ntt
        ev_trans = [None] * ntt
        ev_evac = [None] * ntt
        ev_ffn = [None] * ntt
        ev_r2 = [None] * ntt
        ev_stats2 = [None] * ntt
        ev_sq2 = [None] * ntt
        ev_rstd2 = [None] * ntt
        ev_ln2 = [None] * ntt
        ev_store = [None] * ntt

        def plan_quant(g):
            par = g % 2
            q_waits = [ev_qsub[g - 2]] if g >= 2 else [ev_iscale]
            ev_quant[g] = p.add("scalar", lambda e, S, g=g, par=par:
                                e.activation(
                qtmp[:, par * d:(par + 1) * d], y1[:, g * d:(g + 1) * d],
                AF.Identity, bias=cvecb[:, 1:2], scale=inv128[:, :]),
                waits=q_waits, inc="S", reads=("y1_%d" % g,),
                writes=("tb%d" % par,))

        def plan_qsub(g):
            par = g % 2
            s_waits = [ev_quant[g]]
            if g >= 1:
                s_waits.append(ev_trans[g - 1])
            ev_qsub[g] = p.add("vector", lambda e, S, par=par: e.tensor_scalar(
                qb[:, :], qtmp[:, par * d:(par + 1) * d], -1.0, MAGIC_C,
                AluOpType.mult, AluOpType.add),
                waits=s_waits, inc="V", reads=("tb%d" % par,),
                writes=("qb2",), drain_before=(g == 0))

        def plan_trans(g):
            t_waits = [ev_qsub[g]] + ([("dC2", 32)] if g == 0 else [])
            if g >= 1:
                t_waits.append(ev_evac[g - 1])
            if g == 0:
                t_waits += [ev_bse[(s, nt // 2 - 1)] for s in range(n_seq)]
            def trans_fn(e, S):
                mm = None
                for c in range(nk):
                    dst = pCD[c // 4]
                    mm = e.matmul(dst[:, (c % 4) * 128:(c % 4) * 128 + 128],
                                  qb[:, c * 128:(c + 1) * 128],
                                  idbfb[:, :], start=True, stop=True)
                return mm
            ev_trans[g] = p.add("tensor", trans_fn, waits=t_waits, inc="T")

        def plan_evac(g):
            par = g % 2
            e_waits = [ev_trans[g]]
            if g >= 2:
                e_waits.append(ev_ffn[g - 2])
            def evac_fn(e, S, par=par):
                e.copy(qt[:, par * d:par * d + 512], pC[:, :])
                return e.copy(qt[:, par * d + 512:(par + 1) * d], pD[:, :])
            ev_evac[g] = p.add("scalar", evac_fn, waits=e_waits, inc="S",
                               writes=("qt%d" % par,))

        def plan_ffn(g):
            par = g % 2
            pX = pAB[par]
            if g >= 2:
                f_waits = [ev_evac[g], ev_r2[g - 2]]
            else:
                f_waits = [ev_evac[g], ev_wq_all, ev_seed, ("dC2", 32),
                           ev_r1a[ntt - 2 + g]]
            def ffn_fn(e, S, par=par, pX=pX):
                mm = None
                for h in range(nb):
                    hs = h * 512
                    e.matmul(pX[:, hs:hs + 512], ones2b[0:2, :],
                             rbhl[0:2, hs:hs + 512], start=True, stop=False)
                    for c in range(nk):
                        mm = e.matmul(pX[:, hs:hs + 512],
                                      qt[:, par * d + c * 128:
                                         par * d + (c + 1) * 128],
                                      qwt[:, c * d + hs:c * d + hs + 512],
                                      start=False, stop=(c == nk - 1))
                return mm
            ev_ffn[g] = p.add("tensor", ffn_fn, waits=f_waits, inc="T")

        def plan_r2stats(g):
            par = g % 2
            pX = pAB[par]
            def r2_fn(e, S, par=par, pX=pX, g=g):
                return e.scalar_tensor_tensor(
                    y1[:, g * d:(g + 1) * d], pX[:, :], sout128[:, :],
                    y1[:, g * d:(g + 1) * d], AluOpType.mult, AluOpType.add)
            ev_r2[g] = p.add("vector", r2_fn, waits=[ev_ffn[g]], inc="V",
                             reads=("sout128",), writes=("y1_%d" % g,))
            def stats_fn(e, S, g=g, par=par):
                e.bn_stats(bst[:, 0:6], y1[:, g * d:g * d + 512])
                e.bn_stats(bst[:, 6:12], y1[:, g * d + 512:(g + 1) * d])
                e.drain()
                return e.bn_aggr(mv[:, 2 * par:2 * par + 2], bst[:, :])
            ev_stats2[g] = p.add("vector", stats_fn, waits=(),
                                 inc="V", reads=("y1_%d" % g,),
                                 writes=("mv%d" % par, "bst"))

        def plan_ln2store(g):
            sq, i = divmod(g, nt)
            par = g % 2
            l_waits = [ev_rstd2[g]]
            if g >= 2:
                l_waits.append(ev_store[g - 2])
            else:
                l_waits.append(ev_wq_all)
            ev_ln2[g] = p.add("scalar", lambda e, S, g=g, par=par:
                              e.activation(
                wob[:, par * d:(par + 1) * d], y1[:, g * d:(g + 1) * d],
                AF.Identity, bias=nbb[:, par:par + 1],
                scale=rstdb[:, par:par + 1]),
                waits=l_waits, inc="S",
                reads=("rstd%d" % par, "nb%d" % par, "y1_%d" % g),
                writes=("wob%d" % par,))
            row0 = sq * t_seq + i * 128
            ev_store[g] = p.add("gpsimd", lambda e, S, par=par, row0=row0:
                                e.dma_start(
                out=out[row0:row0 + 128, :],
                in_=wob[:, par * d:(par + 1) * d]),
                waits=[ev_ln2[g]], inc=("dOa" if par == 0 else "dOb"),
                inc_n=16)

        plan_quant(0)
        if ntt > 1:
            plan_quant(1)
        plan_qsub(0)
        for it in range(ntt + 3):
            if it < ntt:
                plan_trans(it)
                plan_evac(it)
            if it + 1 < ntt:
                plan_qsub(it + 1)
            if it + 2 < ntt:
                plan_quant(it + 2)
            if 0 <= it - 1 < ntt:
                plan_ffn(it - 1)
            if 0 <= it - 2 < ntt:
                plan_r2stats(it - 2)
            if 0 <= it - 3 < ntt:
                plan_sqrstd(it - 3, ev_stats2, ev_sq2, ev_rstd2)
                plan_ln2store(it - 3)

        p.add("gpsimd", lambda e, S: e.nop(),
              waits=[ev_store[ntt - 1], ev_store[ntt - 2]])

        sems = {}
        semstack = contextlib.ExitStack()
        with semstack:
            for s in p.cnt:
                sems[s] = semstack.enter_context(nc.semaphore(s))
            with nc.Block() as block:
                p.emit(block, sems)

    return nc


def make_in_maps(x, w, b, t_seq, n_seq, n_cores):
    consts = host_consts(t_seq)
    wt = np.ascontiguousarray(w.T)
    brow = np.ascontiguousarray(b.reshape(1, -1))
    n_batch = x.shape[0]
    per = n_batch // n_cores
    in_maps = []
    for c in range(n_cores):
        xcv = np.ascontiguousarray(
            x[c * per:(c + 1) * per].reshape(per * t_seq, x.shape[2]))
        m = {"xc": xcv, "wt": wt, "bias": brow}
        m.update(consts)
        in_maps.append(m)
    return in_maps


def kernel(L0_attn_x, L0_ffn_weight, L0_ffn_bias):
    x = np.asarray(L0_attn_x, dtype=np.float32)
    w = np.asarray(L0_ffn_weight, dtype=np.float32)
    b = np.asarray(L0_ffn_bias, dtype=np.float32)
    n_batch, t_seq, d = x.shape
    n_cores = 8
    n_seq = n_batch // n_cores
    nc = build_nc(t_seq=t_seq, n_seq=n_seq, d=d, n_cores=n_cores)
    in_maps = make_in_maps(x, w, b, t_seq, n_seq, n_cores)
    res = run_bass_kernel_spmd(nc, in_maps, core_ids=list(range(n_cores)))
    outs = [r["out"].reshape(n_seq, t_seq, d) for r in res.results]
    return np.ascontiguousarray(np.concatenate(outs, axis=0))
